# version 35
# baseline (speedup 1.0000x reference)
import numpy as np
import concourse.bass as bass
import concourse.mybir as mybir
from concourse.bass_utils import run_bass_kernel_spmd

F32 = mybir.dt.float32
BF16 = mybir.dt.bfloat16
AF = mybir.ActivationFunctionType
ALU = mybir.AluOpType

D = 1024
H = 16
DH = 64
NT = D // 128
LN_EPS = 1e-5
L2_EPS = 1e-12
CLIP = 100.0

ENGINES = ("tensor", "vector", "scalar", "gpsimd", "sync")


class _Op:
    __slots__ = ("idx", "eng", "fn", "reads", "writes", "csem", "inc_val",
                 "waits", "same_deps", "tag")

    def __init__(self, idx, eng, fn, reads, writes, csem, tag):
        self.idx = idx
        self.eng = eng
        self.fn = fn
        self.reads = reads
        self.writes = writes
        self.csem = csem
        self.inc_val = None
        self.waits = []
        self.same_deps = []
        self.tag = tag


class Planner:

    def __init__(self, nc):
        self.nc = nc
        self.ops = []
        self.res = {}
        self.eng_sems = {}
        self.dma_counts = {}
        self.family_sems = set()
        self._needs_inc = set()
        self.final_waits = []

    def new_sem(self, name):
        return self.nc.alloc_semaphore(name=name)

    def setup(self):
        for e in ENGINES:
            self.eng_sems[e] = self.new_sem(f"ord_{e}")

    def _r(self, name):
        r = self.res.get(name)
        if r is None:
            r = {"w": None, "readers": []}
            self.res[name] = r
        return r

    def op(self, eng, fn, reads=(), writes=(), csem=None, tag=""):
        o = _Op(len(self.ops), eng, fn, tuple(reads), tuple(writes), csem, tag)
        self.ops.append(o)
        deps = []
        for rn in o.reads:
            r = self._r(rn)
            if r["w"] is not None:
                deps.append(r["w"])
        for wn in o.writes:
            r = self._r(wn)
            if r["w"] is not None:
                deps.append(r["w"])
            deps.extend(r["readers"])
        for rn in o.reads:
            self._r(rn)["readers"].append(o)
        for wn in o.writes:
            r = self._r(wn)
            r["w"] = o
            r["readers"] = []
        seen = set()
        for d in deps:
            if d.idx == o.idx or d.idx in seen:
                continue
            seen.add(d.idx)
            if d.csem is not None:
                o.waits.append((d.csem, 16 * self.dma_counts[d.csem.name]))
            elif d.eng == o.eng:
                if d.eng != "tensor":
                    o.same_deps.append(d)
            else:
                self._needs_inc.add(d.idx)
                o.waits.append(d)
        return o

    def dma(self, eng, fn, csem, reads=(), writes=(), tag=""):
        o = self.op(eng, fn, reads=reads, writes=writes, csem=csem, tag=tag)
        n = self.dma_counts.get(csem.name, 0) + 1
        self.dma_counts[csem.name] = n
        o.inc_val = 16 * n
        return o

    def resolve(self):
        counts = {e: 0 for e in ENGINES}
        for o in self.ops:
            if o.csem is None and o.idx in self._needs_inc:
                counts[o.eng] += 1
                o.inc_val = counts[o.eng]

    def emit(self, block):
        planner = self

        def make_body(eng_name):
            def body(eng):
                marks = {}
                emit_seq = {}
                seq = 0
                drained_upto = -1
                for o in planner.ops:
                    if o.eng != eng_name:
                        continue
                    emit_seq[o.idx] = seq
                    seq += 1
                    need_waits = {}
                    for d in o.waits:
                        if isinstance(d, tuple):
                            sem, val = d
                        else:
                            sem = planner.eng_sems[d.eng]
                            val = d.inc_val
                        assert val is not None, (o.tag,)
                        if need_waits.get(sem.name, (None, 0))[1] < val:
                            need_waits[sem.name] = (sem, val)
                    for sem, val in need_waits.values():
                        if marks.get(sem.name, 0) < val:
                            eng.wait_ge(sem, val)
                            marks[sem.name] = val
                    need = -1
                    for d in o.same_deps:
                        s_ = emit_seq.get(d.idx, -1)
                        if s_ > need:
                            need = s_
                    if need > drained_upto:
                        eng.drain()
                        drained_upto = seq - 2
                    ins = o.fn(eng)
                    if o.inc_val is not None:
                        if o.csem is not None:
                            ins.then_inc(o.csem, 16)
                        else:
                            ins.then_inc(planner.eng_sems[o.eng], 1)
                if eng_name == "sync":
                    for sem, val in planner.final_waits:
                        eng.wait_ge(sem, val)
            return body

        for e in ENGINES:
            getattr(block, e)(make_body(e))


def build_program(NB=2, L=2048, trunc=None):
    SPAN = 512
    CK = 256
    NSPAN = L // SPAN
    nc = bass.Bass("TRN2", target_bir_lowering=False)

    x_hbm = nc.dram_tensor("x", [NB, L, D], F32, kind="ExternalInput").ap()
    wq_hbm = nc.dram_tensor("wq", [D, D], F32, kind="ExternalInput").ap()
    wk_hbm = nc.dram_tensor("wk", [D, D], F32, kind="ExternalInput").ap()
    wv_hbm = nc.dram_tensor("wv", [D, D], F32, kind="ExternalInput").ap()
    wo_hbm = nc.dram_tensor("wo", [D, D], F32, kind="ExternalInput").ap()
    wb_hbm = nc.dram_tensor("wb", [D, H], F32, kind="ExternalInput").ap()
    bb_hbm = nc.dram_tensor("bb", [1, H], F32, kind="ExternalInput").ap()
    wf_hbm = nc.dram_tensor("wf", [D, D], F32, kind="ExternalInput").ap()
    fsc_hbm = nc.dram_tensor("fsc", [1, D], F32, kind="ExternalInput").ap()
    fbi_hbm = nc.dram_tensor("fbi", [1, D], F32, kind="ExternalInput").ap()
    out_hbm = nc.dram_tensor("out", [NB, L, D], F32, kind="ExternalOutput").ap()

    sb = {}

    def sbuf(name, shape, dtype):
        sb[name] = nc.alloc_sbuf_tensor(name, shape, dtype)

    sbuf("Wq", [128, NT * D], BF16)
    sbuf("Wk", [128, NT * D], BF16)
    sbuf("Wv", [128, NT * D], BF16)
    sbuf("Wo", [128, NT * D], BF16)
    sbuf("Wb", [128, NT * H], BF16)
    sbuf("signT", [128, NT * D], BF16)
    sbuf("bb_row", [1, H], BF16)
    sbuf("bias_row", [1, D], BF16)
    sbuf("ones_row", [1, 512], BF16)
    sbuf("zero_row", [1, 512], BF16)
    sbuf("ident", [128, 128], BF16)
    sbuf("mask01", [128, 2 * CK], BF16)

    sbuf("ind_dt", [128, H * NT], BF16)
    sbuf("ind16", [16, 128 * (H // 2)], BF16)

    sbuf("scale_pp", [128, NT], F32)
    sbuf("scale2_pp", [128, NT], F32)

    sbuf("xf32", [128, 4 * D], F32)
    sbuf("xn_nat", [128, 4 * D], BF16)
    sbuf("bn", [128, 48], F32)
    sbuf("mv", [128, 8], F32)
    sbuf("sd", [128, 4], F32)
    sbuf("rstd", [128, 4], F32)
    sbuf("murstd", [128, 4], F32)
    sbuf("xnT", [128, 2 * NT * SPAN], BF16)
    sbuf("qT", [128, NT * SPAN], BF16)
    sbuf("kT", [128, NT * SPAN], BF16)
    sbuf("n2q", [16, SPAN], F32)
    sbuf("rnqT", [16, SPAN], BF16)
    sbuf("n2k", [16, SPAN], F32)
    sbuf("rnkT", [16, SPAN], BF16)
    sbuf("betaT", [16, SPAN], F32)
    sbuf("fkbT", [16, SPAN], BF16)
    sbuf("fkb", [128, 4 * H], BF16)
    sbuf("vt", [128, 4 * D], BF16)
    sbuf("k_nat", [128, 4 * D], BF16)
    sbuf("ctxT", [128, 2 * (H // 2) * CK], BF16)
    sbuf("sT", [128, 8 * CK], BF16)
    sbuf("Spad", [128, 2 * (H // 2) * 128], BF16)
    sbuf("fb", [128, 4 * CK], F32)
    sbuf("outt", [128, 4 * D], F32)
    sb["sq"] = sb["sT"][:, 0:2 * SPAN]
    sb["iota16"] = sb["xf32"][0:16, 0:1024]
    sb["iota_dt"] = sb["xf32"][:, D:D + 128]
    sb["iota"] = sb["xf32"][:, D + 128:D + 128 + CK]
    sb["wf_f32"] = sb["outt"][:, 0:D]
    sbuf("sgn_buf", [128, 2 * SPAN], BF16)
    sb["sgn_bf"] = sb["sgn_buf"][:, :]

    ps = {}
    for name, shape, dt_ in [
        ("pS", [128, 512], F32),
        ("pBig0", [128, 512], F32),
        ("pBig1", [128, 512], F32),
        ("pBig2", [128, 512], F32),
        ("T0", [128, 512], F32),
        ("T1", [128, 512], F32),
        ("T2", [128, 512], F32),
        ("T3", [128, 512], F32),
    ]:
        ps[name] = nc.alloc_psum_tensor(name, shape, dt_)

    tpool = ["T0", "T1", "T2", "T3"]
    tp_gen = [0]

    def new_gen():
        nm = tpool[tp_gen[0] % 4]
        tp_gen[0] += 1
        return nm

    def tv_bf(bank, slot):
        return ps[bank][:, slot * 64:(slot + 1) * 64].bitcast(BF16)

    def tv_bf16w(bank, slot):
        return ps[bank][:, slot * 8:(slot + 1) * 8].bitcast(BF16)

    P = Planner(nc)
    P.setup()
    cs_x = [P.new_sem(f"cx{i}") for i in range(4)]
    cs_wf = [P.new_sem("cwf0"), P.new_sem("cwf1")]
    cs_wfam = {n: P.new_sem(f"cw_{n}") for n in ["Wq", "Wk", "Wv", "Wo", "Wb", "misc"]}
    cs_acc = [P.new_sem(f"cacc{i}") for i in range(4)]
    cs_tx = [P.new_sem(f"ctx{i}") for i in range(2)]
    cs_tk = [P.new_sem(f"ctk{i}") for i in range(2)]
    cs_tf = P.new_sem("ctf")
    cs_ts = P.new_sem("cts")
    cs_st = [P.new_sem(f"cst{i}") for i in range(4)]

    P.op("gpsimd", lambda e: e.iota(sb["iota"][:], [[1, CK]], base=0,
                                    channel_multiplier=-1,
                                    allow_small_or_imprecise_dtypes=True),
         writes=["xf32.1"], tag="iota")
    P.op("vector", lambda e: e.tensor_scalar(sb["mask01"][:, 0:CK], sb["iota"][:], 0.0, None, ALU.is_ge),
         reads=["xf32.1"], writes=["mask01"], tag="mask0")
    P.op("vector", lambda e: e.tensor_scalar(sb["mask01"][:, CK:2 * CK], sb["iota"][:], 128.0, None, ALU.is_ge),
         reads=["xf32.1", "mask01"], writes=["mask01"], tag="mask1")
    P.op("vector", lambda e: e.tensor_scalar(sb["ident"][:], sb["iota"][:, 0:128], 0.0, None, ALU.is_equal),
         reads=["xf32.1"], writes=["ident"], tag="ident")
    P.op("vector", lambda e: e.memset(sb["ones_row"][:], 1.0), writes=["ones_row"], tag="ones")
    P.op("vector", lambda e: e.memset(sb["zero_row"][:], 0.0), writes=["zero_row"], tag="zeros")
    P.op("vector", lambda e: e.memset(sb["Spad"][:], 0.0), writes=["SpadZ"], tag="spadz")
    P.op("gpsimd", lambda e: e.iota(sb["iota_dt"][:], [[-2, NT], [1, H]], base=0,
                                    channel_multiplier=0,
                                    allow_small_or_imprecise_dtypes=True),
         writes=["xf32.1"], tag="iota_dt")
    P.op("vector", lambda e: e.tensor_scalar(sb["ind_dt"][0:64, :], sb["iota_dt"][0:64, :],
                                             0.0, None, ALU.is_equal),
         reads=["xf32.1"], writes=["ind_dt"], tag="inddt_lo")
    P.op("vector", lambda e: e.tensor_scalar(sb["ind_dt"][64:128, :], sb["iota_dt"][64:128, :],
                                             1.0, None, ALU.is_equal),
         reads=["xf32.1", "ind_dt"], writes=["ind_dt"], tag="inddt_hi")
    P.op("gpsimd", lambda e: e.iota(sb["iota16"][0:16, :], [[2, H // 2], [1, 2], [0, 64]], base=0,
                                    channel_multiplier=-1,
                                    allow_small_or_imprecise_dtypes=True),
         writes=["xf32.0"], tag="iota16")
    P.op("vector", lambda e: e.tensor_scalar(sb["ind16"][:], sb["iota16"][0:16, :],
                                             0.0, None, ALU.is_equal),
         reads=["xf32.0"], writes=["ind16"], tag="ind16_c")

    def wload(dst, c0, c1, src_ap, res, tag, fam):
        P.dma("gpsimd", (lambda dst, c0, c1, src_ap:
                         lambda e: e.dma_start(sb[dst][:, c0:c1], src_ap))(dst, c0, c1, src_ap),
              cs_wfam[fam], writes=[res], tag=tag)

    for wname, hbm in [("Wq", wq_hbm), ("Wk", wk_hbm), ("Wv", wv_hbm)]:
        for t in range(NT):
            wload(wname, t * D, (t + 1) * D, hbm[t * 128:(t + 1) * 128, :],
                  f"{wname}.{t}", f"{wname.lower()}{t}", wname)
    for t in range(NT):
        wload("Wb", t * H, (t + 1) * H, wb_hbm[t * 128:(t + 1) * 128, :], f"Wb.{t}", f"wb{t}", "Wb")
    wload("bb_row", 0, H, bb_hbm, "bb_row", "bb", "misc")
    wload("bias_row", 0, D, fbi_hbm, "bias_row", "fbias", "misc")
    for t in range(NT):
        wload("Wo", t * D, (t + 1) * D, wo_hbm[t * 128:(t + 1) * 128, :],
              f"Wo.{t}", f"wo{t}", "Wo")
    for t in range(NT):
        P.dma("gpsimd",
              (lambda t: lambda e: e.dma_start(
                  sb["scale_pp"][:, t:t + 1],
                  fsc_hbm[0:1, t * 128:(t + 1) * 128].rearrange("o (p c) -> (o p) c", c=1)))(t),
              cs_wfam["misc"], writes=[f"scale_pp.{t}"], tag=f"fsc{t}")
    P.op("vector", lambda e: e.tensor_scalar(sb["scale2_pp"][:], sb["scale_pp"][:], 2.0, None, ALU.mult),
         reads=[f"scale_pp.{t}" for t in range(NT)], writes=["scale2_pp"], tag="sc2")

    sgn_slot = [sb["sgn_bf"], sb["outt"][:, 3 * D:3 * D + 512].bitcast(BF16)]
    sgn_res = ["sgn_buf", "outt.3.0"]

    def emit_sign_dve(ot):
        r = 1 + (ot % 2)
        s = ot % 2
        P.dma("gpsimd",
              (lambda ot, r: lambda e: e.dma_start(
                  sb["outt"][:, r * D:(r + 1) * D], wf_hbm[ot * 128:(ot + 1) * 128, :]))(ot, r),
              cs_wf[ot % 2], writes=[f"outt.{r}.0", f"outt.{r}.1"], tag=f"wf{ot}")
        P.op("vector",
             (lambda ot, r, s: lambda e: e.tensor_scalar(
                 sgn_slot[s][:, :], sb["outt"][:, r * D:(r + 1) * D],
                 0.0, None, ALU.is_ge))(ot, r, s),
             reads=[f"outt.{r}.0", f"outt.{r}.1"], writes=[sgn_res[s]], tag=f"sgn{ot}")
        P.op("vector",
             (lambda ot, s: lambda e: e.tensor_scalar(
                 sgn_slot[s][:, :], sgn_slot[s][:, :],
                 sb["scale2_pp"][:, ot:ot + 1],
                 sb["scale_pp"][:, ot:ot + 1],
                 ALU.mult, ALU.subtract))(ot, s),
             reads=[sgn_res[s], f"scale_pp.{ot}", "scale2_pp"],
             writes=[sgn_res[s]], tag=f"sgs{ot}")

    def emit_sign_pe(ot):
        s = ot % 2
        g = new_gen()
        for ib in range(NT):
            P.op("tensor",
                 (lambda ot, ib, g, s: lambda e: e.transpose(
                     tv_bf(g, ib),
                     sgn_slot[s][:, ib * 128:(ib + 1) * 128], sb["ident"][:]))(ot, ib, g, s),
                 reads=[sgn_res[s], "ident"], writes=[g], tag=f"str{ot}.{ib}")
        for ib in range(NT):
            P.op("scalar",
                 (lambda ot, ib, g: lambda e: e.copy(
                     sb["signT"][:, ib * D + ot * 128: ib * D + (ot + 1) * 128],
                     tv_bf(g, ib)))(ot, ib, g),
                 reads=[g], writes=[f"signT.{ib}.{ot}"], tag=f"sev{ot}.{ib}")

    big_rot = ["pBig0", "pBig1", "pBig2"]
    big_i = [0]

    def next_big():
        nm = big_rot[big_i[0] % 3]
        big_i[0] += 1
        return nm

    def w_slice(w, di, c0, c1):
        return sb[w][:, di * D + c0: di * D + c1]

    st_i = [0]
    fb_i = [0]
    x_i = [0]
    o_i = [0]
    pending_D = [[]]

    sgn_dve = [(lambda ot: lambda: emit_sign_dve(ot))(_ot) for _ot in range(NT)]

    def pop_D(n=1):
        for _ in range(n):
            if pending_D[0]:
                pending_D[0].pop(0)()
            if sgn_dve:
                sgn_dve.pop(0)()

    def flush_D():
        while pending_D[0]:
            pending_D[0].pop(0)()

    for _ot in range(NT):
        pending_D[0].append((lambda ot: lambda: emit_sign_pe(ot))(_ot))
    store_total = {0: 0, 1: 0, 2: 0, 3: 0}

    for b in range(NB):
        P.op("vector", lambda e: e.memset(ps["pS"][:], 0.0), reads=[], writes=["pS"], tag=f"zS{b}")

        def emit_ln_loads(b_, s_):
            for lt in range(4):
                row0 = s_ * SPAN + lt * 128
                P.dma("sync",
                      (lambda b_, row0, lt: lambda e: e.dma_start(
                          sb["xf32"][:, lt * D:(lt + 1) * D], x_hbm[b_, row0:row0 + 128, :]))(b_, row0, lt),
                      cs_x[lt], writes=[f"xf32.{lt}"], tag=f"x{b_}.{row0}")

        def emit_ln_stats(b_, s_):
            for lt in range(4):
                P.op("vector",
                     (lambda lt: lambda e: e.bn_stats(sb["bn"][:, lt * 12: lt * 12 + 6],
                                                      sb["xf32"][:, lt * D: lt * D + 512]))(lt),
                     reads=[f"xf32.{lt}"], writes=[f"bn.{lt}.0"], tag="bn0")
                P.op("vector",
                     (lambda lt: lambda e: e.bn_stats(sb["bn"][:, lt * 12 + 6: lt * 12 + 12],
                                                      sb["xf32"][:, lt * D + 512: lt * D + 1024]))(lt),
                     reads=[f"xf32.{lt}"], writes=[f"bn.{lt}.1"], tag="bn1")
            for lt in range(4):
                P.op("vector",
                     (lambda lt: lambda e: e.bn_aggr(sb["mv"][:, 2 * lt:2 * lt + 2],
                                                     sb["bn"][:, lt * 12: lt * 12 + 12]))(lt),
                     reads=[f"bn.{lt}.0", f"bn.{lt}.1"], writes=[f"mv.{lt}"], tag="bnag")

        def emit_ln_finish(b_, s_):
            P.op("vector", lambda e: e.tensor_scalar(
                sb["sd"][:].rearrange("p (t c) -> p t c", c=1),
                sb["mv"][:].rearrange("p (t c) -> p t c", c=2)[:, :, 1:2],
                LN_EPS, None, ALU.add),
                 reads=[f"mv.{lt_}" for lt_ in range(4)], writes=["sd"], tag="veps")
            P.op("scalar", lambda e: e.activation(sb["sd"][:], sb["sd"][:], AF.Ln),
                 reads=["sd"], writes=["sd"], tag="ln")
            P.op("scalar", lambda e: e.activation(sb["rstd"][:], sb["sd"][:], AF.Exp, scale=-0.5),
                 reads=["sd"], writes=["rstd"], tag="exp")
            P.op("vector", lambda e: e.tensor_tensor(
                sb["murstd"][:].rearrange("p (t c) -> p t c", c=1),
                sb["mv"][:].rearrange("p (t c) -> p t c", c=2)[:, :, 0:1],
                sb["rstd"][:].rearrange("p (t c) -> p t c", c=1), ALU.mult),
                 reads=[f"mv.{lt_}" for lt_ in range(4)] + ["rstd"],
                 writes=["murstd"], tag="murstd")
        def emit_ln_affines(b_, s_):
            for lt in range(4):
                P.op("vector",
                     (lambda lt: lambda e: e.tensor_scalar(
                         sb["xn_nat"][:, lt * D:(lt + 1) * D],
                         sb["xf32"][:, lt * D:(lt + 1) * D],
                         sb["rstd"][:, lt:lt + 1], sb["murstd"][:, lt:lt + 1],
                         ALU.mult, ALU.subtract))(lt),
                     reads=[f"xf32.{lt}", "rstd", "murstd"], writes=[f"xn_nat.{lt}"], tag="affine")

        def emit_ln_transposes(sp_):
            for lt in range(4):
                g = new_gen()
                for db in range(NT):
                    P.op("tensor",
                         (lambda lt, db, g: lambda e: e.transpose(
                             tv_bf(g, db),
                             sb["xn_nat"][:, lt * D + db * 128: lt * D + (db + 1) * 128],
                             sb["ident"][:]))(lt, db, g),
                         reads=[f"xn_nat.{lt}", "ident"], writes=[g], tag="xnT_t")
                P.op("scalar",
                     (lambda sp_, lt, g: lambda e: e.copy(
                         sb["xnT"][:, sp_ * NT * SPAN:(sp_ + 1) * NT * SPAN]
                         .rearrange("p (d l) -> p d l", l=SPAN)[:, :, lt * 128:(lt + 1) * 128],
                         ps[g][:, :].bitcast(BF16).rearrange("p (d c) -> p d c", c=128)))(sp_, lt, g),
                     reads=[g], writes=[f"xnT.{sp_}.{db_}.{lt}" for db_ in range(NT)], tag="xnT_e")

        if b == 0:
            emit_ln_loads(0, 0)
            emit_ln_stats(0, 0)
            emit_ln_finish(0, 0)
            emit_ln_affines(0, 0)
            emit_ln_transposes(0)

        for s in range(NSPAN):
            gspan = b * NSPAN + s
            sp = gspan % 2
            if gspan + 1 < NB * NSPAN:
                emit_ln_loads(*divmod(gspan + 1, NSPAN))

            for wname, dstT, n2name, rnname in (("Wq", "qT", "n2q", "rnqT"),
                                                ("Wk", "kT", "n2k", "rnkT")):
                pn = new_gen()
                nm_pend = []
                for dt_ in range(NT):
                    pb = next_big()
                    for di in range(NT):
                        P.op("tensor",
                             (lambda wname, dt_, di, pb, sp: lambda e: e.matmul(
                                 ps[pb][:, :],
                                 w_slice(wname, di, dt_ * 128, (dt_ + 1) * 128),
                                 sb["xnT"][:, (sp * NT + di) * SPAN:(sp * NT + di + 1) * SPAN],
                                 start=(di == 0), stop=(di == NT - 1)))(wname, dt_, di, pb, sp),
                             reads=[f"{wname}.{di}"] + [f"xnT.{sp}.{di}.{lt}" for lt in range(4)],
                             writes=[pb], tag=f"{dstT}mm{dt_}.{di}")
                    P.op("scalar",
                         (lambda dstT, dt_, pb: lambda e: e.copy(
                             sb[dstT][:, dt_ * SPAN:(dt_ + 1) * SPAN], ps[pb][:, :]))(dstT, dt_, pb),
                         reads=[pb], writes=[f"{dstT}.{dt_}"], tag=f"{dstT}ev{dt_}")
                    sqs = dt_ % 2
                    P.op("vector",
                         (lambda sqs, dstT, dt_: lambda e: e.tensor_tensor(
                             sb["sq"][:, sqs * SPAN:(sqs + 1) * SPAN],
                             sb[dstT][:, dt_ * SPAN:(dt_ + 1) * SPAN],
                             sb[dstT][:, dt_ * SPAN:(dt_ + 1) * SPAN], ALU.mult))(sqs, dstT, dt_),
                         reads=[f"{dstT}.{dt_}"],
                         writes=[f"sT.{2 * sqs}", f"sT.{2 * sqs + 1}"], tag=f"sq{dt_}")
                    def nm_emit(dt_, sqs, pn=pn):
                        P.op("tensor",
                             (lambda dt_, sqs, pn: lambda e: e.matmul(
                                 ps[pn][0:16, :],
                                 sb["ind_dt"][:, dt_ * H:(dt_ + 1) * H],
                                 sb["sq"][:, sqs * SPAN:(sqs + 1) * SPAN],
                                 start=(dt_ == 0), stop=(dt_ == NT - 1)))(dt_, sqs, pn),
                             reads=[f"sT.{2 * sqs}", f"sT.{2 * sqs + 1}", "ind_dt"], writes=[pn], tag=f"nm{dt_}")
                    nm_pend.append((dt_, sqs))
                    if len(nm_pend) > 1:
                        nm_emit(*nm_pend.pop(0))
                for job in nm_pend:
                    nm_emit(*job)
                P.op("scalar",
                     (lambda n2name, pn: lambda e: e.activation(
                         sb[n2name][:, :], ps[pn][0:16, :], AF.Ln))(n2name, pn),
                     reads=[pn], writes=[n2name], tag=f"{n2name}_l")
                P.op("scalar",
                     (lambda n2name: lambda e: e.activation(
                         sb[n2name][:, :], sb[n2name][:, :], AF.Exp, scale=-0.5))(n2name),
                     reads=[n2name], writes=[n2name], tag=f"{n2name}_e")
                P.op("vector",
                     (lambda n2name, rnname: lambda e: e.tensor_scalar(
                         sb[rnname][:, :], sb[n2name][:, :], 1.0 / L2_EPS, None,
                         ALU.min))(n2name, rnname),
                     reads=[n2name], writes=[rnname], tag=rnname)


            pending_lnfin = None
            if gspan + 1 < NB * NSPAN:
                emit_ln_stats(*divmod(gspan + 1, NSPAN))
                pending_lnfin = divmod(gspan + 1, NSPAN)

            gbt = new_gen()
            P.op("tensor",
                 (lambda gbt: lambda e: e.matmul(ps[gbt][0:16, :], sb["bb_row"][:, :],
                                                 sb["ones_row"][:, :], start=True, stop=False))(gbt),
                 reads=["ones_row", "bb_row"], writes=[gbt], tag="bseed")
            for di in range(NT):
                P.op("tensor",
                     (lambda di, gbt, sp: lambda e: e.matmul(
                         ps[gbt][0:16, :],
                         sb["Wb"][:, di * H:(di + 1) * H],
                         sb["xnT"][:, (sp * NT + di) * SPAN:(sp * NT + di + 1) * SPAN],
                         start=False, stop=(di == NT - 1)))(di, gbt, sp),
                     reads=[f"Wb.{di}"] + [f"xnT.{sp}.{di}.{lt_}" for lt_ in range(4)],
                     writes=[gbt], tag=f"bmm{di}")
            P.op("scalar",
                 (lambda gbt: lambda e: e.activation(
                     sb["betaT"][:, :], ps[gbt][0:16, :], AF.Exp, scale=-1.0))(gbt),
                 reads=[gbt], writes=["betaT"], tag="bexp1")
            def emit_sig_tail(step):
                if step == 0:
                    P.op("scalar",
                         lambda e: e.activation(sb["betaT"][:, :], sb["betaT"][:, :],
                                                AF.Ln, bias=1.0),
                         reads=["betaT"], writes=["betaT"], tag="bln")
                else:
                    P.op("scalar",
                         lambda e: e.activation(sb["betaT"][:, :], sb["betaT"][:, :],
                                                AF.Exp, scale=-1.0),
                         reads=["betaT"], writes=["betaT"], tag="bexp2")
                    P.op("vector",
                         lambda e: e.tensor_tensor(sb["fkbT"][:, :], sb["betaT"][:, :],
                                                   sb["rnkT"][:, :], ALU.mult),
                         reads=["betaT", "rnkT"], writes=["fkbT"], tag="fkbT")

            vjobs = [(lt, dc) for lt in range(4) for dc in range(2)]
            vpend = []

            def emit_fkb_unused():
                gfk = new_gen()
                for lt in range(4):
                    P.op("tensor",
                         (lambda lt, gfk: lambda e: e.transpose(
                             tv_bf16w(gfk, lt),
                             sb["fkbT"][:, lt * 128:(lt + 1) * 128],
                             sb["ident"][0:16, 0:16]))(lt, gfk),
                         reads=["fkbT", "ident"], writes=[gfk], tag=f"fkT{lt}")
                for lt in range(4):
                    P.op("scalar",
                         (lambda lt, gfk: lambda e: e.copy(
                             sb["fkb"][:, lt * H:(lt + 1) * H],
                             tv_bf16w(gfk, lt)))(lt, gfk),
                         reads=[gfk], writes=[f"fkb.{lt}"], tag=f"fkE{lt}")

            def emit_vevac(lt, dc, pb):
                P.op("scalar",
                     (lambda lt, dc, pb: lambda e: e.copy(
                         sb["vt"][:, lt * D + dc * 512: lt * D + (dc + 1) * 512],
                         ps[pb][:, :]))(lt, dc, pb),
                     reads=[pb], writes=[f"vt.{lt}.{dc}"], tag=f"vev{lt}.{dc}")

            for i, (lt, dc) in enumerate(vjobs):
                if i >= 3:
                    emit_vevac(*vpend.pop(0))
                if i in (4, 5):
                    emit_sig_tail(i - 4)
                if i == 6 and pending_lnfin is not None:
                    emit_ln_finish(*pending_lnfin)
                if i in (6, 7) and gspan == 0 and sgn_dve:
                    sgn_dve.pop(0)()
                pb = next_big()
                for di in range(NT):
                    P.op("tensor",
                         (lambda lt, dc, di, pb, sp: lambda e: e.matmul(
                             ps[pb][:, :],
                             sb["xnT"][:, (sp * NT + di) * SPAN + lt * 128:
                                       (sp * NT + di) * SPAN + (lt + 1) * 128],
                             w_slice("Wv", di, dc * 512, (dc + 1) * 512),
                             start=(di == 0), stop=(di == NT - 1)))(lt, dc, di, pb, sp),
                         reads=[f"xnT.{sp}.{di}.{lt}", f"Wv.{di}"], writes=[pb], tag=f"vmm{lt}.{dc}")
                vpend.append((lt, dc, pb))
            for j in vpend:
                emit_vevac(*j)

            gfk = new_gen()
            for lt in range(4):
                P.op("tensor",
                     (lambda lt, gfk: lambda e: e.transpose(
                         tv_bf16w(gfk, lt),
                         sb["fkbT"][:, lt * 128:(lt + 1) * 128],
                         sb["ident"][0:16, 0:16]))(lt, gfk),
                     reads=["fkbT", "ident"], writes=[gfk], tag=f"fkT{lt}")
            for lt in range(4):
                P.op("scalar",
                     (lambda lt, gfk: lambda e: e.copy(
                         sb["fkb"][:, lt * H:(lt + 1) * H],
                         tv_bf16w(gfk, lt)))(lt, gfk),
                     reads=[gfk], writes=[f"fkb.{lt}"], tag=f"fkE{lt}")
            for lt in range(4):
                for dc in range(2):
                    P.op("vector",
                         (lambda lt, dc: lambda e: e.tensor_tensor(
                             sb["vt"][:, lt * D + dc * 512: lt * D + (dc + 1) * 512]
                             .rearrange("p (a c) -> p a c", c=DH),
                             sb["vt"][:, lt * D + dc * 512: lt * D + (dc + 1) * 512]
                             .rearrange("p (a c) -> p a c", c=DH),
                             sb["fkb"][:, lt * H + dc * 8: lt * H + (dc + 1) * 8]
                             .unsqueeze(2).broadcast_to([128, 8, DH]), ALU.mult))(lt, dc),
                         reads=[f"vt.{lt}.{dc}", f"fkb.{lt}"],
                         writes=[f"vt.{lt}.{dc}"], tag=f"vfold{lt}.{dc}")

            if pending_lnfin is not None:
                emit_ln_affines(*pending_lnfin)

            for ci in range(2):
                c = 2 * s + ci
                cs_ = c % 2
                first_chunk = (c == 0)

                for tb in range(2):
                    g = new_gen()
                    for db in range(NT):
                        P.op("tensor",
                             (lambda ci, tb, db, g: lambda e: e.transpose(
                                 tv_bf(g, db),
                                 sb["kT"][:, db * SPAN + ci * CK + tb * 128:
                                           db * SPAN + ci * CK + (tb + 1) * 128],
                                 sb["ident"][:]))(ci, tb, db, g),
                             reads=[f"kT.{db}", "ident"], writes=[g], tag="knT")
                    P.op("scalar",
                         (lambda cs_, tb, g: lambda e: e.copy(
                             sb["k_nat"][:, (cs_ * 2 + tb) * D:(cs_ * 2 + tb + 1) * D],
                             ps[g][:, :].bitcast(BF16)))(cs_, tb, g),
                         reads=[g], writes=[f"k_nat.{cs_}.{tb}.{db_}" for db_ in range(NT)],
                         tag="knE")

                def cxgroup(p_, st_base, fbs, ci=ci, cs_=cs_, first_chunk=first_chunk):
                    gcx = new_gen()
                    for h01 in range(2):
                        h = 2 * p_ + h01
                        for tb in range(2):
                            sts = st_base + h01 * 2 + tb
                            P.op("tensor",
                                 (lambda h, h01, tb, ci, sts, gcx, first_chunk: lambda e: e.matmul(
                                     ps[gcx][64 * h01:64 * h01 + 64, 0:CK],
                                     sb["vt"][:, (ci * 2 + tb) * D + h * DH:
                                              (ci * 2 + tb) * D + (h + 1) * DH],
                                     sb["sT"][:, sts * CK:(sts + 1) * CK],
                                     start=(tb == 0),
                                     stop=(first_chunk and tb == 1)))(h, h01, tb, ci, sts, gcx, first_chunk),
                                 reads=[f"vt.{ci * 2 + tb}.{h // 8}", f"sT.{sts}"],
                                 writes=[gcx], tag=f"cx{h}.{tb}")
                        if not first_chunk:
                            P.op("tensor",
                                 (lambda p_, h01, ci, cs_, gcx: lambda e: e.matmul(
                                     ps[gcx][64 * h01:64 * h01 + 64, 0:CK],
                                     sb["Spad"][:, ((1 - cs_) * (H // 2) + p_) * 128 + 64 * h01:
                                                ((1 - cs_) * (H // 2) + p_) * 128 + 64 * h01 + 64],
                                     sb["qT"][:, p_ * SPAN + ci * CK: p_ * SPAN + (ci + 1) * CK],
                                     start=False, stop=True))(p_, h01, ci, cs_, gcx),
                                 reads=[f"Spad.{1 - cs_}.{2 * p_}", f"Spad.{1 - cs_}.{2 * p_ + 1}",
                                        "SpadZ", f"qT.{p_}"],
                                 writes=[gcx], tag=f"in{p_}.{h01}")

                    for tb in range(2):
                        for h01 in range(2):
                            h = 2 * p_ + h01
                            P.op("tensor",
                                 (lambda h, tb, ci, cs_: lambda e: e.matmul(
                                     ps["pS"][64 * (h % 2):64 * (h % 2) + 64,
                                              64 * (h // 2):64 * (h // 2) + 64],
                                     sb["k_nat"][:, (cs_ * 2 + tb) * D + h * DH:
                                                 (cs_ * 2 + tb) * D + (h + 1) * DH],
                                     sb["vt"][:, (ci * 2 + tb) * D + h * DH:
                                              (ci * 2 + tb) * D + (h + 1) * DH],
                                     start=False, stop=True, skip_group_check=True))(h, tb, ci, cs_),
                                 reads=[f"k_nat.{cs_}.{tb}.{h // 2}", f"vt.{ci * 2 + tb}.{h // 8}"],
                                 writes=["pS"], tag=f"st{h}.{tb}")

                    P.op("vector",
                         (lambda p_, cs_, fbs, gcx: lambda e: e.tensor_tensor(
                             sb["ctxT"][:, (cs_ * (H // 2) + p_) * CK:(cs_ * (H // 2) + p_ + 1) * CK],
                             ps[gcx][:, 0:CK],
                             sb["fb"][:, fbs * 2 * CK + (p_ % 2) * CK:
                                       fbs * 2 * CK + (p_ % 2 + 1) * CK], ALU.mult))(p_, cs_, fbs, gcx),
                         reads=[gcx, f"fb.{fbs}.{p_ % 2}"],
                         writes=[f"ctxT.{cs_}.{p_}.0", f"ctxT.{cs_}.{p_}.1"], tag=f"ce{p_}")

                prev_pair = [None]
                for p_ in range(H // 2):
                    pop_D(1)
                    if p_ % 2 == 0:
                        gfb = new_gen()
                        fbs = fb_i[0] % 2
                        fb_i[0] += 1
                        for j in range(2):
                            P.op("tensor",
                                 (lambda p_, j, ci, gfb: lambda e: e.matmul(
                                     ps[gfb][:, j * CK:(j + 1) * CK],
                                     sb["ind16"][:, (p_ + j) * 128:(p_ + j + 1) * 128],
                                     sb["rnqT"][:, ci * CK:(ci + 1) * CK],
                                     start=True, stop=True))(p_, j, ci, gfb),
                                 reads=["ind16", "rnqT"], writes=[gfb], tag=f"fbm{p_ + j}")
                        P.op("scalar",
                             (lambda fbs, gfb: lambda e: e.copy(
                                 sb["fb"][:, fbs * 2 * CK:(fbs + 1) * 2 * CK],
                                 ps[gfb][:, :]))(fbs, gfb),
                             reads=[gfb], writes=[f"fb.{fbs}.0", f"fb.{fbs}.1"], tag=f"fbe{p_}")
                    st_base = (p_ % 2) * 4
                    gscs = [new_gen(), new_gen()]

                    def sc_emit(h01, tb):
                        h = 2 * p_ + h01
                        gsc = gscs[h01]
                        P.op("tensor",
                             (lambda h, tb, ci, gsc: lambda e: e.matmul(
                                 ps[gsc][:, tb * CK:(tb + 1) * CK],
                                 sb["kT"][64 * (h % 2):64 * (h % 2) + 64,
                                          (h // 2) * SPAN + ci * CK + tb * 128:
                                          (h // 2) * SPAN + ci * CK + (tb + 1) * 128],
                                 sb["qT"][64 * (h % 2):64 * (h % 2) + 64,
                                          (h // 2) * SPAN + ci * CK:
                                          (h // 2) * SPAN + (ci + 1) * CK],
                                 start=True, stop=True))(h, tb, ci, gsc),
                             reads=[f"kT.{h // 2}", f"qT.{h // 2}"],
                             writes=[gsc], tag=f"sc{h}.{tb}")

                    sc_emit(0, 0)
                    sc_emit(1, 0)
                    sc_emit(0, 1)
                    sc_emit(1, 1)
                    for h01 in range(2):
                        P.op("vector",
                             (lambda h01, st_base, gsc=gscs[h01]: lambda e: e.tensor_tensor(
                                 sb["sT"][:, (st_base + h01 * 2) * CK:(st_base + h01 * 2 + 2) * CK],
                                 ps[gsc][:, :],
                                 sb["mask01"][:], ALU.mult))(h01, st_base),
                             reads=[gscs[h01], "mask01"],
                             writes=[f"sT.{st_base + h01 * 2}", f"sT.{st_base + h01 * 2 + 1}"],
                             tag=f"se{p_}.{h01}")

                    if prev_pair[0] is not None:
                        cxgroup(*prev_pair[0])
                    prev_pair[0] = (p_, st_base, fbs)
                cxgroup(*prev_pair[0])

                for par in range(2):
                    P.op("scalar",
                         (lambda par, cs_: lambda e: e.copy(
                             sb["Spad"][64 * par:64 * par + 64,
                                        cs_ * (H // 2) * 128:(cs_ + 1) * (H // 2) * 128]
                             .rearrange("p (a c) -> p a c", c=128)[:, :, 64 * par:64 * par + 64],
                             ps["pS"][64 * par:64 * par + 64, :]
                             .rearrange("p (a c) -> p a c", c=64)))(par, cs_),
                         reads=["pS"],
                         writes=[f"Spad.{cs_}.{2 * pp_ + par}" for pp_ in range(H // 2)],
                         tag=f"sv{par}")

                flush_D()

                dseg = []

                def make_d(lt, oslot, row0, cs_, span_lt, sp, b):
                    def emit_wo(dc, pw):
                        for pt in range(H // 2):
                            P.op("tensor",
                                 (lambda cs_, pt, lt, dc, pw: lambda e: e.matmul(
                                     ps[pw][:, :],
                                     sb["ctxT"][:, (cs_ * (H // 2) + pt) * CK + lt * 128:
                                                (cs_ * (H // 2) + pt) * CK + (lt + 1) * 128],
                                     w_slice("Wo", pt, dc * 512, (dc + 1) * 512),
                                     start=(pt == 0), stop=(pt == H // 2 - 1)))(cs_, pt, lt, dc, pw),
                                 reads=[f"ctxT.{cs_}.{pt}.0", f"ctxT.{cs_}.{pt}.1", f"Wo.{pt}"],
                                 writes=[pw], tag=f"woM{lt}{dc}")

                    def emit_y(dc, pyp):
                        P.op("tensor",
                             (lambda dc, pyp: lambda e: e.matmul(
                                 ps[pyp][:, :], sb["ones_row"][:, 0:128],
                                 sb["bias_row"][:, dc * 512:(dc + 1) * 512],
                                 start=True, stop=False))(dc, pyp),
                             reads=["ones_row", "bias_row"], writes=[pyp], tag=f"ys{lt}{dc}")
                        for di in range(NT):
                            P.op("tensor",
                                 (lambda di, dc, span_lt, pyp, sp: lambda e: e.matmul(
                                     ps[pyp][:, :],
                                     sb["xnT"][:, (sp * NT + di) * SPAN + span_lt * 128:
                                               (sp * NT + di) * SPAN + (span_lt + 1) * 128],
                                     sb["signT"][:, di * D + dc * 512: di * D + (dc + 1) * 512],
                                     start=False, stop=(di == NT - 1)))(di, dc, span_lt, pyp, sp),
                                 reads=[f"xnT.{sp}.{di}.{span_lt}"] +
                                       [f"signT.{di}.{ot}" for ot in range(NT)],
                                 writes=[pyp], tag=f"yM{lt}{dc}{di}")

                    def emit_comb(dc, pw, pyp):
                        P.op("vector",
                             (lambda oslot, dc, pyp: lambda e: e.tensor_scalar(
                                 sb["outt"][:, oslot * D + dc * 512: oslot * D + (dc + 1) * 512],
                                 ps[pyp][:, :], CLIP, -CLIP, ALU.min, ALU.max))(oslot, dc, pyp),
                             reads=[pyp], writes=[f"outt.{oslot}.{dc}"], tag=f"cl{lt}{dc}")
                        P.op("vector",
                             (lambda oslot, dc, pw: lambda e: e.tensor_tensor(
                                 sb["outt"][:, oslot * D + dc * 512: oslot * D + (dc + 1) * 512],
                                 sb["outt"][:, oslot * D + dc * 512: oslot * D + (dc + 1) * 512],
                                 ps[pw][:, :], ALU.add))(oslot, dc, pw),
                             reads=[pw, f"outt.{oslot}.{dc}"],
                             writes=[f"outt.{oslot}.{dc}"], tag=f"ad{lt}{dc}")

                    def tail():
                        P.dma("gpsimd",
                              (lambda b, row0, oslot: lambda e: e.dma_start(
                                  sb["outt"][:, oslot * D:(oslot + 1) * D],
                                  x_hbm[b, row0:row0 + 128, :], accum_op=ALU.add))(b, row0, oslot),
                              cs_acc[oslot],
                              reads=[f"outt.{oslot}.0", f"outt.{oslot}.1"],
                              writes=[f"outt.{oslot}.0", f"outt.{oslot}.1"], tag=f"acc{b}.{row0}")
                        P.dma("sync",
                              (lambda b, row0, oslot: lambda e: e.dma_start(
                                  out_hbm[b, row0:row0 + 128, :],
                                  sb["outt"][:, oslot * D:(oslot + 1) * D]))(b, row0, oslot),
                              cs_st[oslot],
                              reads=[f"outt.{oslot}.0", f"outt.{oslot}.1"], tag=f"st{b}.{row0}")
                        store_total[oslot] += 1

                    pws = {}

                    def s1():
                        pws["pw0"] = next_big()
                        emit_wo(0, pws["pw0"])

                    def s2():
                        pws["py0"] = next_big()
                        emit_y(0, pws["py0"])

                    def s3():
                        pws["pw1"] = next_big()
                        emit_wo(1, pws["pw1"])
                        emit_comb(0, pws["pw0"], pws["py0"])

                    def s4():
                        pws["py1"] = next_big()
                        emit_y(1, pws["py1"])
                        emit_comb(1, pws["pw1"], pws["py1"])
                        tail()
                    return [s1, s2, s3, s4]

                for lt in range(2):
                    oslot = o_i[0] % 4
                    o_i[0] += 1
                    dseg.extend(make_d(lt, oslot, c * CK + lt * 128, cs_, ci * 2 + lt, sp, b))

                def defer_ln(ci=ci, gspan=gspan):
                    if ci == 0 and gspan + 1 < NB * NSPAN:
                        emit_ln_transposes((gspan + 1) % 2)
                dseg.insert(0, defer_ln)
                pending_D[0] = pending_D[0] + dseg

    flush_D()

    if trunc is not None:
        P.ops = P.ops[:trunc]
        P.dma_counts = {}
        for o in P.ops:
            if o.csem is not None:
                P.dma_counts[o.csem.name] = P.dma_counts.get(o.csem.name, 0) + 1
    P.resolve()
    for i in range(4):
        n = P.dma_counts.get(cs_st[i].name, 0)
        if n:
            P.final_waits.append((cs_st[i], 16 * n))

    with nc.allow_low_precision(reason="bf16 matmul operand rounding"):
        with nc.Block() as block:
            P.emit(block)
    return nc


def kernel(L0_attn_x, L0_attn_Wq, L0_attn_Wk, L0_attn_Wv, L0_attn_Wo,
           L0_attn_Wb, L0_attn_bb, L0_ffn_weights, L0_ffn_scale, L0_ffn_bias):
    B, L, D_ = L0_attn_x.shape
    n_cores = 8
    nb = B // n_cores
    nc = build_program(NB=nb, L=L)

    x = np.ascontiguousarray(np.asarray(L0_attn_x, dtype=np.float32))
    common = dict(
        wq=np.ascontiguousarray(L0_attn_Wq, dtype=np.float32),
        wk=np.ascontiguousarray(L0_attn_Wk, dtype=np.float32),
        wv=np.ascontiguousarray(L0_attn_Wv, dtype=np.float32),
        wo=np.ascontiguousarray(L0_attn_Wo, dtype=np.float32),
        wb=np.ascontiguousarray(L0_attn_Wb, dtype=np.float32),
        bb=np.ascontiguousarray(np.asarray(L0_attn_bb, dtype=np.float32).reshape(1, H)),
        wf=np.ascontiguousarray(L0_ffn_weights, dtype=np.float32),
        fsc=np.ascontiguousarray(np.asarray(L0_ffn_scale, dtype=np.float32).reshape(1, D)),
        fbi=np.ascontiguousarray(np.asarray(L0_ffn_bias, dtype=np.float32).reshape(1, D)),
    )
    in_maps = []
    for i in range(n_cores):
        m = dict(common)
        m["x"] = np.ascontiguousarray(x[i * nb:(i + 1) * nb])
        in_maps.append(m)

    res = run_bass_kernel_spmd(nc, in_maps, core_ids=list(range(n_cores)))
    return np.concatenate([r["out"] for r in res.results], axis=0)


# version 36 (final)
# speedup vs baseline: 1.0620x; 1.0620x over previous
import numpy as np
import concourse.bass as bass
import concourse.mybir as mybir
from concourse.bass_utils import run_bass_kernel_spmd

F32 = mybir.dt.float32
BF16 = mybir.dt.bfloat16
AF = mybir.ActivationFunctionType
ALU = mybir.AluOpType

D = 1024
H = 16
DH = 64
NT = D // 128
LN_EPS = 1e-5
L2_EPS = 1e-12
CLIP = 100.0

ENGINES = ("tensor", "vector", "scalar", "gpsimd", "sync")


class _Op:
    __slots__ = ("idx", "eng", "fn", "reads", "writes", "csem", "inc_val",
                 "waits", "same_deps", "tag")

    def __init__(self, idx, eng, fn, reads, writes, csem, tag):
        self.idx = idx
        self.eng = eng
        self.fn = fn
        self.reads = reads
        self.writes = writes
        self.csem = csem
        self.inc_val = None
        self.waits = []
        self.same_deps = []
        self.tag = tag


class Planner:

    def __init__(self, nc):
        self.nc = nc
        self.ops = []
        self.res = {}
        self.eng_sems = {}
        self.dma_counts = {}
        self.family_sems = set()
        self._needs_inc = set()
        self.final_waits = []

    def new_sem(self, name):
        return self.nc.alloc_semaphore(name=name)

    def setup(self):
        for e in ENGINES:
            self.eng_sems[e] = self.new_sem(f"ord_{e}")

    def _r(self, name):
        r = self.res.get(name)
        if r is None:
            r = {"w": None, "readers": []}
            self.res[name] = r
        return r

    def op(self, eng, fn, reads=(), writes=(), csem=None, tag=""):
        o = _Op(len(self.ops), eng, fn, tuple(reads), tuple(writes), csem, tag)
        self.ops.append(o)
        deps = []
        for rn in o.reads:
            r = self._r(rn)
            if r["w"] is not None:
                deps.append(r["w"])
        for wn in o.writes:
            r = self._r(wn)
            if r["w"] is not None:
                deps.append(r["w"])
            deps.extend(r["readers"])
        for rn in o.reads:
            self._r(rn)["readers"].append(o)
        for wn in o.writes:
            r = self._r(wn)
            r["w"] = o
            r["readers"] = []
        seen = set()
        for d in deps:
            if d.idx == o.idx or d.idx in seen:
                continue
            seen.add(d.idx)
            if d.csem is not None:
                o.waits.append((d.csem, 16 * self.dma_counts[d.csem.name]))
            elif d.eng == o.eng:
                if d.eng != "tensor":
                    o.same_deps.append(d)
            else:
                self._needs_inc.add(d.idx)
                o.waits.append(d)
        return o

    def dma(self, eng, fn, csem, reads=(), writes=(), tag=""):
        o = self.op(eng, fn, reads=reads, writes=writes, csem=csem, tag=tag)
        n = self.dma_counts.get(csem.name, 0) + 1
        self.dma_counts[csem.name] = n
        o.inc_val = 16 * n
        return o

    def resolve(self):
        counts = {e: 0 for e in ENGINES}
        for o in self.ops:
            if o.csem is None and o.idx in self._needs_inc:
                counts[o.eng] += 1
                o.inc_val = counts[o.eng]

    def emit(self, block):
        planner = self

        def make_body(eng_name):
            def body(eng):
                marks = {}
                emit_seq = {}
                seq = 0
                drained_upto = -1
                for o in planner.ops:
                    if o.eng != eng_name:
                        continue
                    emit_seq[o.idx] = seq
                    seq += 1
                    need_waits = {}
                    for d in o.waits:
                        if isinstance(d, tuple):
                            sem, val = d
                        else:
                            sem = planner.eng_sems[d.eng]
                            val = d.inc_val
                        assert val is not None, (o.tag,)
                        if need_waits.get(sem.name, (None, 0))[1] < val:
                            need_waits[sem.name] = (sem, val)
                    for sem, val in need_waits.values():
                        if marks.get(sem.name, 0) < val:
                            eng.wait_ge(sem, val)
                            marks[sem.name] = val
                    need = -1
                    for d in o.same_deps:
                        s_ = emit_seq.get(d.idx, -1)
                        if s_ > need:
                            need = s_
                    if need > drained_upto:
                        eng.drain()
                        drained_upto = seq - 2
                    ins = o.fn(eng)
                    if o.inc_val is not None:
                        if o.csem is not None:
                            ins.then_inc(o.csem, 16)
                        else:
                            ins.then_inc(planner.eng_sems[o.eng], 1)
                if eng_name == "sync":
                    for sem, val in planner.final_waits:
                        eng.wait_ge(sem, val)
            return body

        for e in ENGINES:
            getattr(block, e)(make_body(e))


def build_program(NB=2, L=2048, trunc=None):
    SPAN = 512
    CK = 256
    NSPAN = L // SPAN
    nc = bass.Bass("TRN2", target_bir_lowering=False)

    x_hbm = nc.dram_tensor("x", [NB, L, D], F32, kind="ExternalInput").ap()
    wq_hbm = nc.dram_tensor("wq", [D, D], F32, kind="ExternalInput").ap()
    wk_hbm = nc.dram_tensor("wk", [D, D], F32, kind="ExternalInput").ap()
    wv_hbm = nc.dram_tensor("wv", [D, D], F32, kind="ExternalInput").ap()
    wo_hbm = nc.dram_tensor("wo", [D, D], F32, kind="ExternalInput").ap()
    wb_hbm = nc.dram_tensor("wb", [D, H], F32, kind="ExternalInput").ap()
    bb_hbm = nc.dram_tensor("bb", [1, H], F32, kind="ExternalInput").ap()
    wf_hbm = nc.dram_tensor("wf", [D, D], F32, kind="ExternalInput").ap()
    fsc_hbm = nc.dram_tensor("fsc", [1, D], F32, kind="ExternalInput").ap()
    fbi_hbm = nc.dram_tensor("fbi", [1, D], F32, kind="ExternalInput").ap()
    out_hbm = nc.dram_tensor("out", [NB, L, D], F32, kind="ExternalOutput").ap()

    sb = {}

    def sbuf(name, shape, dtype):
        sb[name] = nc.alloc_sbuf_tensor(name, shape, dtype)

    sbuf("Wq", [128, NT * D], BF16)
    sbuf("Wk", [128, NT * D], BF16)
    sbuf("Wv", [128, NT * D], BF16)
    sbuf("Wo", [128, NT * D], BF16)
    sbuf("Wb", [128, NT * H], BF16)
    sbuf("signT", [128, NT * D], BF16)
    sbuf("bb_row", [1, H], BF16)
    sbuf("bias_row", [1, D], BF16)
    sbuf("ones_row", [1, 512], BF16)
    sbuf("zero_row", [1, 512], BF16)
    sbuf("ident", [128, 128], BF16)
    sbuf("mask01", [128, 2 * CK], BF16)

    sbuf("ind_dt", [128, H * NT], BF16)
    sbuf("ind16", [16, 128 * (H // 2)], BF16)

    sbuf("scale_pp", [128, NT], F32)
    sbuf("scale2_pp", [128, NT], F32)

    sbuf("xf32", [128, 4 * D], F32)
    sbuf("xn_nat", [128, 4 * D], BF16)
    sbuf("bn", [128, 48], F32)
    sbuf("mv", [128, 8], F32)
    sbuf("sd", [128, 4], F32)
    sbuf("rstd", [128, 4], F32)
    sbuf("murstd", [128, 4], F32)
    sbuf("xnT", [128, 2 * NT * SPAN], BF16)
    sbuf("qT", [128, NT * SPAN], BF16)
    sbuf("kT", [128, NT * SPAN], BF16)
    sbuf("n2q", [16, SPAN], F32)
    sbuf("rnqT", [16, SPAN], BF16)
    sbuf("n2k", [16, SPAN], F32)
    sbuf("rnkT", [16, SPAN], BF16)
    sbuf("betaT", [16, SPAN], F32)
    sbuf("fkbT", [16, SPAN], BF16)
    sbuf("fkb", [128, 4 * H], BF16)
    sbuf("vt", [128, 4 * D], BF16)
    sbuf("k_nat", [128, 4 * D], BF16)
    sbuf("ctxT", [128, 2 * (H // 2) * CK], BF16)
    sbuf("sT", [128, 8 * CK], BF16)
    sbuf("Spad", [128, 2 * (H // 2) * 128], BF16)
    sbuf("fb", [128, 4 * CK], F32)
    sbuf("outt", [128, 4 * D], F32)
    sb["sq"] = sb["sT"][:, 0:2 * SPAN]
    sb["iota16"] = sb["xf32"][0:16, 0:1024]
    sb["iota_dt"] = sb["xf32"][:, D:D + 128]
    sb["iota"] = sb["xf32"][:, D + 128:D + 128 + CK]
    sb["wf_f32"] = sb["outt"][:, 0:D]
    sbuf("sgn_buf", [128, 2 * SPAN], BF16)
    sb["sgn_bf"] = sb["sgn_buf"][:, :]

    ps = {}
    for name, shape, dt_ in [
        ("pS", [128, 512], F32),
        ("pBig0", [128, 512], F32),
        ("pBig1", [128, 512], F32),
        ("pBig2", [128, 512], F32),
        ("T0", [128, 512], F32),
        ("T1", [128, 512], F32),
        ("T2", [128, 512], F32),
        ("T3", [128, 512], F32),
    ]:
        ps[name] = nc.alloc_psum_tensor(name, shape, dt_)

    tpool = ["T0", "T1", "T2", "T3"]
    tp_gen = [0]

    def new_gen():
        nm = tpool[tp_gen[0] % 4]
        tp_gen[0] += 1
        return nm

    def tv_bf(bank, slot):
        return ps[bank][:, slot * 64:(slot + 1) * 64].bitcast(BF16)

    def tv_bf16w(bank, slot):
        return ps[bank][:, slot * 8:(slot + 1) * 8].bitcast(BF16)

    P = Planner(nc)
    P.setup()
    cs_x = [P.new_sem(f"cx{i}") for i in range(4)]
    cs_wf = [P.new_sem("cwf0"), P.new_sem("cwf1")]
    cs_wfam = {n: P.new_sem(f"cw_{n}") for n in ["Wq", "Wk", "Wv", "Wo", "Wb", "misc"]}
    cs_acc = [P.new_sem(f"cacc{i}") for i in range(4)]
    cs_tx = [P.new_sem(f"ctx{i}") for i in range(2)]
    cs_tk = [P.new_sem(f"ctk{i}") for i in range(2)]
    cs_tf = P.new_sem("ctf")
    cs_ts = P.new_sem("cts")
    cs_st = [P.new_sem(f"cst{i}") for i in range(4)]

    P.op("gpsimd", lambda e: e.iota(sb["iota"][:], [[1, CK]], base=0,
                                    channel_multiplier=-1,
                                    allow_small_or_imprecise_dtypes=True),
         writes=["xf32.1"], tag="iota")
    P.op("vector", lambda e: e.tensor_scalar(sb["mask01"][:, 0:CK], sb["iota"][:], 0.0, None, ALU.is_ge),
         reads=["xf32.1"], writes=["mask01"], tag="mask0")
    P.op("vector", lambda e: e.tensor_scalar(sb["mask01"][:, CK:2 * CK], sb["iota"][:], 128.0, None, ALU.is_ge),
         reads=["xf32.1", "mask01"], writes=["mask01"], tag="mask1")
    P.op("vector", lambda e: e.tensor_scalar(sb["ident"][:], sb["iota"][:, 0:128], 0.0, None, ALU.is_equal),
         reads=["xf32.1"], writes=["ident"], tag="ident")
    P.op("vector", lambda e: e.memset(sb["ones_row"][:], 1.0), writes=["ones_row"], tag="ones")
    P.op("vector", lambda e: e.memset(sb["zero_row"][:], 0.0), writes=["zero_row"], tag="zeros")
    P.op("vector", lambda e: e.memset(sb["Spad"][:], 0.0), writes=["SpadZ"], tag="spadz")
    P.op("gpsimd", lambda e: e.iota(sb["iota_dt"][:], [[-2, NT], [1, H]], base=0,
                                    channel_multiplier=0,
                                    allow_small_or_imprecise_dtypes=True),
         writes=["xf32.1"], tag="iota_dt")
    P.op("vector", lambda e: e.tensor_scalar(sb["ind_dt"][0:64, :], sb["iota_dt"][0:64, :],
                                             0.0, None, ALU.is_equal),
         reads=["xf32.1"], writes=["ind_dt"], tag="inddt_lo")
    P.op("vector", lambda e: e.tensor_scalar(sb["ind_dt"][64:128, :], sb["iota_dt"][64:128, :],
                                             1.0, None, ALU.is_equal),
         reads=["xf32.1", "ind_dt"], writes=["ind_dt"], tag="inddt_hi")
    P.op("gpsimd", lambda e: e.iota(sb["iota16"][0:16, :], [[2, H // 2], [1, 2], [0, 64]], base=0,
                                    channel_multiplier=-1,
                                    allow_small_or_imprecise_dtypes=True),
         writes=["xf32.0"], tag="iota16")
    P.op("vector", lambda e: e.tensor_scalar(sb["ind16"][:], sb["iota16"][0:16, :],
                                             0.0, None, ALU.is_equal),
         reads=["xf32.0"], writes=["ind16"], tag="ind16_c")

    def wload(dst, c0, c1, src_ap, res, tag, fam):
        P.dma("gpsimd", (lambda dst, c0, c1, src_ap:
                         lambda e: e.dma_start(sb[dst][:, c0:c1], src_ap))(dst, c0, c1, src_ap),
              cs_wfam[fam], writes=[res], tag=tag)

    for wname, hbm in [("Wq", wq_hbm), ("Wk", wk_hbm), ("Wv", wv_hbm)]:
        for t in range(NT):
            wload(wname, t * D, (t + 1) * D, hbm[t * 128:(t + 1) * 128, :],
                  f"{wname}.{t}", f"{wname.lower()}{t}", wname)
    for t in range(NT):
        wload("Wb", t * H, (t + 1) * H, wb_hbm[t * 128:(t + 1) * 128, :], f"Wb.{t}", f"wb{t}", "Wb")
    wload("bb_row", 0, H, bb_hbm, "bb_row", "bb", "misc")
    wload("bias_row", 0, D, fbi_hbm, "bias_row", "fbias", "misc")
    for t in range(NT):
        wload("Wo", t * D, (t + 1) * D, wo_hbm[t * 128:(t + 1) * 128, :],
              f"Wo.{t}", f"wo{t}", "Wo")
    for t in range(NT):
        P.dma("gpsimd",
              (lambda t: lambda e: e.dma_start(
                  sb["scale_pp"][:, t:t + 1],
                  fsc_hbm[0:1, t * 128:(t + 1) * 128].rearrange("o (p c) -> (o p) c", c=1)))(t),
              cs_wfam["misc"], writes=[f"scale_pp.{t}"], tag=f"fsc{t}")
    P.op("vector", lambda e: e.tensor_scalar(sb["scale2_pp"][:], sb["scale_pp"][:], 2.0, None, ALU.mult),
         reads=[f"scale_pp.{t}" for t in range(NT)], writes=["scale2_pp"], tag="sc2")

    sgn_slot = [sb["sgn_bf"], sb["outt"][:, 3 * D:3 * D + 512].bitcast(BF16)]
    sgn_res = ["sgn_buf", "outt.3.0"]

    def emit_sign_dve(ot):
        r = 1 + (ot % 2)
        s = ot % 2
        P.dma("gpsimd",
              (lambda ot, r: lambda e: e.dma_start(
                  sb["outt"][:, r * D:(r + 1) * D], wf_hbm[ot * 128:(ot + 1) * 128, :]))(ot, r),
              cs_wf[ot % 2], writes=[f"outt.{r}.0", f"outt.{r}.1"], tag=f"wf{ot}")
        P.op("vector",
             (lambda ot, r, s: lambda e: e.tensor_scalar(
                 sgn_slot[s][:, :], sb["outt"][:, r * D:(r + 1) * D],
                 0.0, None, ALU.is_ge))(ot, r, s),
             reads=[f"outt.{r}.0", f"outt.{r}.1"], writes=[sgn_res[s]], tag=f"sgn{ot}")
        P.op("vector",
             (lambda ot, s: lambda e: e.tensor_scalar(
                 sgn_slot[s][:, :], sgn_slot[s][:, :],
                 sb["scale2_pp"][:, ot:ot + 1],
                 sb["scale_pp"][:, ot:ot + 1],
                 ALU.mult, ALU.subtract))(ot, s),
             reads=[sgn_res[s], f"scale_pp.{ot}", "scale2_pp"],
             writes=[sgn_res[s]], tag=f"sgs{ot}")

    def emit_sign_pe(ot):
        s = ot % 2
        g = new_gen()
        for ib in range(NT):
            P.op("tensor",
                 (lambda ot, ib, g, s: lambda e: e.transpose(
                     tv_bf(g, ib),
                     sgn_slot[s][:, ib * 128:(ib + 1) * 128], sb["ident"][:]))(ot, ib, g, s),
                 reads=[sgn_res[s], "ident"], writes=[g], tag=f"str{ot}.{ib}")
        for ib in range(NT):
            P.op("scalar",
                 (lambda ot, ib, g: lambda e: e.copy(
                     sb["signT"][:, ib * D + ot * 128: ib * D + (ot + 1) * 128],
                     tv_bf(g, ib)))(ot, ib, g),
                 reads=[g], writes=[f"signT.{ib}.{ot}"], tag=f"sev{ot}.{ib}")

    big_rot = ["pBig0", "pBig1", "pBig2"]
    big_i = [0]

    def next_big():
        nm = big_rot[big_i[0] % 3]
        big_i[0] += 1
        return nm

    def w_slice(w, di, c0, c1):
        return sb[w][:, di * D + c0: di * D + c1]

    st_i = [0]
    fb_i = [0]
    x_i = [0]
    o_i = [0]
    pending_D = [[]]

    sgn_dve = [(lambda ot: lambda: emit_sign_dve(ot))(_ot) for _ot in range(NT)]

    def pop_D(n=1):
        for _ in range(n):
            if pending_D[0]:
                pending_D[0].pop(0)()
            if sgn_dve:
                sgn_dve.pop(0)()

    def flush_D():
        while pending_D[0]:
            pending_D[0].pop(0)()

    for _ot in range(NT):
        pending_D[0].append((lambda ot: lambda: emit_sign_pe(ot))(_ot))
    store_total = {0: 0, 1: 0, 2: 0, 3: 0}

    for b in range(NB):
        P.op("vector", lambda e: e.memset(ps["pS"][:], 0.0), reads=[], writes=["pS"], tag=f"zS{b}")

        def emit_ln_loads(b_, s_):
            for lt in range(4):
                row0 = s_ * SPAN + lt * 128
                P.dma("sync",
                      (lambda b_, row0, lt: lambda e: e.dma_start(
                          sb["xf32"][:, lt * D:(lt + 1) * D], x_hbm[b_, row0:row0 + 128, :]))(b_, row0, lt),
                      cs_x[lt], writes=[f"xf32.{lt}"], tag=f"x{b_}.{row0}")

        def emit_ln_stats(b_, s_):
            for lt in range(4):
                P.op("vector",
                     (lambda lt: lambda e: e.bn_stats(sb["bn"][:, lt * 12: lt * 12 + 6],
                                                      sb["xf32"][:, lt * D: lt * D + 512]))(lt),
                     reads=[f"xf32.{lt}"], writes=[f"bn.{lt}.0"], tag="bn0")
                P.op("vector",
                     (lambda lt: lambda e: e.bn_stats(sb["bn"][:, lt * 12 + 6: lt * 12 + 12],
                                                      sb["xf32"][:, lt * D + 512: lt * D + 1024]))(lt),
                     reads=[f"xf32.{lt}"], writes=[f"bn.{lt}.1"], tag="bn1")
            for lt in range(4):
                P.op("vector",
                     (lambda lt: lambda e: e.bn_aggr(sb["mv"][:, 2 * lt:2 * lt + 2],
                                                     sb["bn"][:, lt * 12: lt * 12 + 12]))(lt),
                     reads=[f"bn.{lt}.0", f"bn.{lt}.1"], writes=[f"mv.{lt}"], tag="bnag")

        def emit_ln_finish(b_, s_):
            P.op("vector", lambda e: e.tensor_scalar(
                sb["sd"][:].rearrange("p (t c) -> p t c", c=1),
                sb["mv"][:].rearrange("p (t c) -> p t c", c=2)[:, :, 1:2],
                LN_EPS, None, ALU.add),
                 reads=[f"mv.{lt_}" for lt_ in range(4)], writes=["sd"], tag="veps")
            P.op("scalar", lambda e: e.activation(sb["sd"][:], sb["sd"][:], AF.Ln),
                 reads=["sd"], writes=["sd"], tag="ln")
            P.op("scalar", lambda e: e.activation(sb["rstd"][:], sb["sd"][:], AF.Exp, scale=-0.5),
                 reads=["sd"], writes=["rstd"], tag="exp")
            P.op("vector", lambda e: e.tensor_tensor(
                sb["murstd"][:].rearrange("p (t c) -> p t c", c=1),
                sb["mv"][:].rearrange("p (t c) -> p t c", c=2)[:, :, 0:1],
                sb["rstd"][:].rearrange("p (t c) -> p t c", c=1), ALU.mult),
                 reads=[f"mv.{lt_}" for lt_ in range(4)] + ["rstd"],
                 writes=["murstd"], tag="murstd")
            for lt in range(4):
                P.op("vector",
                     (lambda lt: lambda e: e.tensor_scalar(
                         sb["xn_nat"][:, lt * D:(lt + 1) * D],
                         sb["xf32"][:, lt * D:(lt + 1) * D],
                         sb["rstd"][:, lt:lt + 1], sb["murstd"][:, lt:lt + 1],
                         ALU.mult, ALU.subtract))(lt),
                     reads=[f"xf32.{lt}", "rstd", "murstd"], writes=[f"xn_nat.{lt}"], tag="affine")

        def emit_ln_transposes(sp_):
            for lt in range(4):
                g = new_gen()
                for db in range(NT):
                    P.op("tensor",
                         (lambda lt, db, g: lambda e: e.transpose(
                             tv_bf(g, db),
                             sb["xn_nat"][:, lt * D + db * 128: lt * D + (db + 1) * 128],
                             sb["ident"][:]))(lt, db, g),
                         reads=[f"xn_nat.{lt}", "ident"], writes=[g], tag="xnT_t")
                P.op("scalar",
                     (lambda sp_, lt, g: lambda e: e.copy(
                         sb["xnT"][:, sp_ * NT * SPAN:(sp_ + 1) * NT * SPAN]
                         .rearrange("p (d l) -> p d l", l=SPAN)[:, :, lt * 128:(lt + 1) * 128],
                         ps[g][:, :].bitcast(BF16).rearrange("p (d c) -> p d c", c=128)))(sp_, lt, g),
                     reads=[g], writes=[f"xnT.{sp_}.{db_}.{lt}" for db_ in range(NT)], tag="xnT_e")

        if b == 0:
            emit_ln_loads(0, 0)
            emit_ln_stats(0, 0)
            emit_ln_finish(0, 0)
            emit_ln_transposes(0)

        for s in range(NSPAN):
            gspan = b * NSPAN + s
            sp = gspan % 2
            if gspan + 1 < NB * NSPAN:
                emit_ln_loads(*divmod(gspan + 1, NSPAN))

            for wname, dstT, n2name, rnname in (("Wq", "qT", "n2q", "rnqT"),
                                                ("Wk", "kT", "n2k", "rnkT")):
                pn = new_gen()
                nm_pend = []
                for dt_ in range(NT):
                    pb = next_big()
                    for di in range(NT):
                        P.op("tensor",
                             (lambda wname, dt_, di, pb, sp: lambda e: e.matmul(
                                 ps[pb][:, :],
                                 w_slice(wname, di, dt_ * 128, (dt_ + 1) * 128),
                                 sb["xnT"][:, (sp * NT + di) * SPAN:(sp * NT + di + 1) * SPAN],
                                 start=(di == 0), stop=(di == NT - 1)))(wname, dt_, di, pb, sp),
                             reads=[f"{wname}.{di}"] + [f"xnT.{sp}.{di}.{lt}" for lt in range(4)],
                             writes=[pb], tag=f"{dstT}mm{dt_}.{di}")
                    P.op("scalar",
                         (lambda dstT, dt_, pb: lambda e: e.copy(
                             sb[dstT][:, dt_ * SPAN:(dt_ + 1) * SPAN], ps[pb][:, :]))(dstT, dt_, pb),
                         reads=[pb], writes=[f"{dstT}.{dt_}"], tag=f"{dstT}ev{dt_}")
                    sqs = dt_ % 2
                    P.op("vector",
                         (lambda sqs, dstT, dt_: lambda e: e.tensor_tensor(
                             sb["sq"][:, sqs * SPAN:(sqs + 1) * SPAN],
                             sb[dstT][:, dt_ * SPAN:(dt_ + 1) * SPAN],
                             sb[dstT][:, dt_ * SPAN:(dt_ + 1) * SPAN], ALU.mult))(sqs, dstT, dt_),
                         reads=[f"{dstT}.{dt_}"],
                         writes=[f"sT.{2 * sqs}", f"sT.{2 * sqs + 1}"], tag=f"sq{dt_}")
                    def nm_emit(dt_, sqs, pn=pn):
                        P.op("tensor",
                             (lambda dt_, sqs, pn: lambda e: e.matmul(
                                 ps[pn][0:16, :],
                                 sb["ind_dt"][:, dt_ * H:(dt_ + 1) * H],
                                 sb["sq"][:, sqs * SPAN:(sqs + 1) * SPAN],
                                 start=(dt_ == 0), stop=(dt_ == NT - 1)))(dt_, sqs, pn),
                             reads=[f"sT.{2 * sqs}", f"sT.{2 * sqs + 1}", "ind_dt"], writes=[pn], tag=f"nm{dt_}")
                    nm_pend.append((dt_, sqs))
                    if len(nm_pend) > 1:
                        nm_emit(*nm_pend.pop(0))
                for job in nm_pend:
                    nm_emit(*job)
                P.op("scalar",
                     (lambda n2name, pn: lambda e: e.activation(
                         sb[n2name][:, :], ps[pn][0:16, :], AF.Ln))(n2name, pn),
                     reads=[pn], writes=[n2name], tag=f"{n2name}_l")
                P.op("scalar",
                     (lambda n2name: lambda e: e.activation(
                         sb[n2name][:, :], sb[n2name][:, :], AF.Exp, scale=-0.5))(n2name),
                     reads=[n2name], writes=[n2name], tag=f"{n2name}_e")
                P.op("vector",
                     (lambda n2name, rnname: lambda e: e.tensor_scalar(
                         sb[rnname][:, :], sb[n2name][:, :], 1.0 / L2_EPS, None,
                         ALU.min))(n2name, rnname),
                     reads=[n2name], writes=[rnname], tag=rnname)


            pending_lnfin = None
            if gspan + 1 < NB * NSPAN:
                emit_ln_stats(*divmod(gspan + 1, NSPAN))
                pending_lnfin = divmod(gspan + 1, NSPAN)

            gbt = new_gen()
            P.op("tensor",
                 (lambda gbt: lambda e: e.matmul(ps[gbt][0:16, :], sb["bb_row"][:, :],
                                                 sb["ones_row"][:, :], start=True, stop=False))(gbt),
                 reads=["ones_row", "bb_row"], writes=[gbt], tag="bseed")
            for di in range(NT):
                P.op("tensor",
                     (lambda di, gbt, sp: lambda e: e.matmul(
                         ps[gbt][0:16, :],
                         sb["Wb"][:, di * H:(di + 1) * H],
                         sb["xnT"][:, (sp * NT + di) * SPAN:(sp * NT + di + 1) * SPAN],
                         start=False, stop=(di == NT - 1)))(di, gbt, sp),
                     reads=[f"Wb.{di}"] + [f"xnT.{sp}.{di}.{lt_}" for lt_ in range(4)],
                     writes=[gbt], tag=f"bmm{di}")
            P.op("scalar",
                 (lambda gbt: lambda e: e.activation(
                     sb["betaT"][:, :], ps[gbt][0:16, :], AF.Exp, scale=-1.0))(gbt),
                 reads=[gbt], writes=["betaT"], tag="bexp1")
            def emit_sig_tail(step):
                if step == 0:
                    P.op("scalar",
                         lambda e: e.activation(sb["betaT"][:, :], sb["betaT"][:, :],
                                                AF.Ln, bias=1.0),
                         reads=["betaT"], writes=["betaT"], tag="bln")
                else:
                    P.op("scalar",
                         lambda e: e.activation(sb["betaT"][:, :], sb["betaT"][:, :],
                                                AF.Exp, scale=-1.0),
                         reads=["betaT"], writes=["betaT"], tag="bexp2")
                    P.op("vector",
                         lambda e: e.tensor_tensor(sb["fkbT"][:, :], sb["betaT"][:, :],
                                                   sb["rnkT"][:, :], ALU.mult),
                         reads=["betaT", "rnkT"], writes=["fkbT"], tag="fkbT")

            vjobs = [(lt, dc) for lt in range(4) for dc in range(2)]
            vpend = []

            def emit_fkb_unused():
                gfk = new_gen()
                for lt in range(4):
                    P.op("tensor",
                         (lambda lt, gfk: lambda e: e.transpose(
                             tv_bf16w(gfk, lt),
                             sb["fkbT"][:, lt * 128:(lt + 1) * 128],
                             sb["ident"][0:16, 0:16]))(lt, gfk),
                         reads=["fkbT", "ident"], writes=[gfk], tag=f"fkT{lt}")
                for lt in range(4):
                    P.op("scalar",
                         (lambda lt, gfk: lambda e: e.copy(
                             sb["fkb"][:, lt * H:(lt + 1) * H],
                             tv_bf16w(gfk, lt)))(lt, gfk),
                         reads=[gfk], writes=[f"fkb.{lt}"], tag=f"fkE{lt}")

            def emit_vevac(lt, dc, pb):
                P.op("scalar",
                     (lambda lt, dc, pb: lambda e: e.copy(
                         sb["vt"][:, lt * D + dc * 512: lt * D + (dc + 1) * 512],
                         ps[pb][:, :]))(lt, dc, pb),
                     reads=[pb], writes=[f"vt.{lt}.{dc}"], tag=f"vev{lt}.{dc}")

            for i, (lt, dc) in enumerate(vjobs):
                if i >= 3:
                    emit_vevac(*vpend.pop(0))
                if i in (4, 5):
                    emit_sig_tail(i - 4)
                if i == 6 and pending_lnfin is not None:
                    emit_ln_finish(*pending_lnfin)
                if i in (6, 7) and gspan == 0 and sgn_dve:
                    sgn_dve.pop(0)()
                pb = next_big()
                for di in range(NT):
                    P.op("tensor",
                         (lambda lt, dc, di, pb, sp: lambda e: e.matmul(
                             ps[pb][:, :],
                             sb["xnT"][:, (sp * NT + di) * SPAN + lt * 128:
                                       (sp * NT + di) * SPAN + (lt + 1) * 128],
                             w_slice("Wv", di, dc * 512, (dc + 1) * 512),
                             start=(di == 0), stop=(di == NT - 1)))(lt, dc, di, pb, sp),
                         reads=[f"xnT.{sp}.{di}.{lt}", f"Wv.{di}"], writes=[pb], tag=f"vmm{lt}.{dc}")
                vpend.append((lt, dc, pb))
            for j in vpend:
                emit_vevac(*j)

            gfk = new_gen()
            for lt in range(4):
                P.op("tensor",
                     (lambda lt, gfk: lambda e: e.transpose(
                         tv_bf16w(gfk, lt),
                         sb["fkbT"][:, lt * 128:(lt + 1) * 128],
                         sb["ident"][0:16, 0:16]))(lt, gfk),
                     reads=["fkbT", "ident"], writes=[gfk], tag=f"fkT{lt}")
            for lt in range(4):
                P.op("scalar",
                     (lambda lt, gfk: lambda e: e.copy(
                         sb["fkb"][:, lt * H:(lt + 1) * H],
                         tv_bf16w(gfk, lt)))(lt, gfk),
                     reads=[gfk], writes=[f"fkb.{lt}"], tag=f"fkE{lt}")
            for lt, dc in ((0, 0), (1, 0), (0, 1), (1, 1),
                           (2, 0), (3, 0), (2, 1), (3, 1)):
                P.op("vector",
                     (lambda lt, dc: lambda e: e.tensor_tensor(
                         sb["vt"][:, lt * D + dc * 512: lt * D + (dc + 1) * 512]
                         .rearrange("p (a c) -> p a c", c=DH),
                         sb["vt"][:, lt * D + dc * 512: lt * D + (dc + 1) * 512]
                         .rearrange("p (a c) -> p a c", c=DH),
                         sb["fkb"][:, lt * H + dc * 8: lt * H + (dc + 1) * 8]
                         .unsqueeze(2).broadcast_to([128, 8, DH]), ALU.mult))(lt, dc),
                     reads=[f"vt.{lt}.{dc}", f"fkb.{lt}"],
                     writes=[f"vt.{lt}.{dc}"], tag=f"vfold{lt}.{dc}")

            for ci in range(2):
                c = 2 * s + ci
                cs_ = c % 2
                first_chunk = (c == 0)

                for tb in range(2):
                    g = new_gen()
                    for db in range(NT):
                        P.op("tensor",
                             (lambda ci, tb, db, g: lambda e: e.transpose(
                                 tv_bf(g, db),
                                 sb["kT"][:, db * SPAN + ci * CK + tb * 128:
                                           db * SPAN + ci * CK + (tb + 1) * 128],
                                 sb["ident"][:]))(ci, tb, db, g),
                             reads=[f"kT.{db}", "ident"], writes=[g], tag="knT")
                    P.op("scalar",
                         (lambda cs_, tb, g: lambda e: e.copy(
                             sb["k_nat"][:, (cs_ * 2 + tb) * D:(cs_ * 2 + tb + 1) * D],
                             ps[g][:, :].bitcast(BF16)))(cs_, tb, g),
                         reads=[g], writes=[f"k_nat.{cs_}.{tb}.{db_}" for db_ in range(NT)],
                         tag="knE")

                def cxgroup(p_, st_base, fbs, ci=ci, cs_=cs_, first_chunk=first_chunk):
                    gcx = new_gen()
                    for h01 in range(2):
                        h = 2 * p_ + h01
                        for tb in range(2):
                            sts = st_base + h01 * 2 + tb
                            P.op("tensor",
                                 (lambda h, h01, tb, ci, sts, gcx, first_chunk: lambda e: e.matmul(
                                     ps[gcx][64 * h01:64 * h01 + 64, 0:CK],
                                     sb["vt"][:, (ci * 2 + tb) * D + h * DH:
                                              (ci * 2 + tb) * D + (h + 1) * DH],
                                     sb["sT"][:, sts * CK:(sts + 1) * CK],
                                     start=(tb == 0),
                                     stop=(first_chunk and tb == 1)))(h, h01, tb, ci, sts, gcx, first_chunk),
                                 reads=[f"vt.{ci * 2 + tb}.{h // 8}", f"sT.{sts}"],
                                 writes=[gcx], tag=f"cx{h}.{tb}")
                        if not first_chunk:
                            P.op("tensor",
                                 (lambda p_, h01, ci, cs_, gcx: lambda e: e.matmul(
                                     ps[gcx][64 * h01:64 * h01 + 64, 0:CK],
                                     sb["Spad"][:, ((1 - cs_) * (H // 2) + p_) * 128 + 64 * h01:
                                                ((1 - cs_) * (H // 2) + p_) * 128 + 64 * h01 + 64],
                                     sb["qT"][:, p_ * SPAN + ci * CK: p_ * SPAN + (ci + 1) * CK],
                                     start=False, stop=True))(p_, h01, ci, cs_, gcx),
                                 reads=[f"Spad.{1 - cs_}.{2 * p_}", f"Spad.{1 - cs_}.{2 * p_ + 1}",
                                        "SpadZ", f"qT.{p_}"],
                                 writes=[gcx], tag=f"in{p_}.{h01}")

                    for tb in range(2):
                        for h01 in range(2):
                            h = 2 * p_ + h01
                            P.op("tensor",
                                 (lambda h, tb, ci, cs_: lambda e: e.matmul(
                                     ps["pS"][64 * (h % 2):64 * (h % 2) + 64,
                                              64 * (h // 2):64 * (h // 2) + 64],
                                     sb["k_nat"][:, (cs_ * 2 + tb) * D + h * DH:
                                                 (cs_ * 2 + tb) * D + (h + 1) * DH],
                                     sb["vt"][:, (ci * 2 + tb) * D + h * DH:
                                              (ci * 2 + tb) * D + (h + 1) * DH],
                                     start=False, stop=True, skip_group_check=True))(h, tb, ci, cs_),
                                 reads=[f"k_nat.{cs_}.{tb}.{h // 2}", f"vt.{ci * 2 + tb}.{h // 8}"],
                                 writes=["pS"], tag=f"st{h}.{tb}")

                    P.op("vector",
                         (lambda p_, cs_, fbs, gcx: lambda e: e.tensor_tensor(
                             sb["ctxT"][:, (cs_ * (H // 2) + p_) * CK:(cs_ * (H // 2) + p_ + 1) * CK],
                             ps[gcx][:, 0:CK],
                             sb["fb"][:, fbs * 2 * CK + (p_ % 2) * CK:
                                       fbs * 2 * CK + (p_ % 2 + 1) * CK], ALU.mult))(p_, cs_, fbs, gcx),
                         reads=[gcx, f"fb.{fbs}.{p_ % 2}"],
                         writes=[f"ctxT.{cs_}.{p_}.0", f"ctxT.{cs_}.{p_}.1"], tag=f"ce{p_}")

                prev_pair = [None]
                for p_ in range(H // 2):
                    pop_D(1)
                    if p_ % 2 == 0:
                        gfb = new_gen()
                        fbs = fb_i[0] % 2
                        fb_i[0] += 1
                        for j in range(2):
                            P.op("tensor",
                                 (lambda p_, j, ci, gfb: lambda e: e.matmul(
                                     ps[gfb][:, j * CK:(j + 1) * CK],
                                     sb["ind16"][:, (p_ + j) * 128:(p_ + j + 1) * 128],
                                     sb["rnqT"][:, ci * CK:(ci + 1) * CK],
                                     start=True, stop=True))(p_, j, ci, gfb),
                                 reads=["ind16", "rnqT"], writes=[gfb], tag=f"fbm{p_ + j}")
                        P.op("scalar",
                             (lambda fbs, gfb: lambda e: e.copy(
                                 sb["fb"][:, fbs * 2 * CK:(fbs + 1) * 2 * CK],
                                 ps[gfb][:, :]))(fbs, gfb),
                             reads=[gfb], writes=[f"fb.{fbs}.0", f"fb.{fbs}.1"], tag=f"fbe{p_}")
                    st_base = (p_ % 2) * 4
                    gscs = [new_gen(), new_gen()]

                    def sc_emit(h01, tb):
                        h = 2 * p_ + h01
                        gsc = gscs[h01]
                        P.op("tensor",
                             (lambda h, tb, ci, gsc: lambda e: e.matmul(
                                 ps[gsc][:, tb * CK:(tb + 1) * CK],
                                 sb["kT"][64 * (h % 2):64 * (h % 2) + 64,
                                          (h // 2) * SPAN + ci * CK + tb * 128:
                                          (h // 2) * SPAN + ci * CK + (tb + 1) * 128],
                                 sb["qT"][64 * (h % 2):64 * (h % 2) + 64,
                                          (h // 2) * SPAN + ci * CK:
                                          (h // 2) * SPAN + (ci + 1) * CK],
                                 start=True, stop=True))(h, tb, ci, gsc),
                             reads=[f"kT.{h // 2}", f"qT.{h // 2}"],
                             writes=[gsc], tag=f"sc{h}.{tb}")

                    sc_emit(0, 0)
                    sc_emit(1, 0)
                    sc_emit(0, 1)
                    sc_emit(1, 1)
                    for h01 in range(2):
                        P.op("vector",
                             (lambda h01, st_base, gsc=gscs[h01]: lambda e: e.tensor_tensor(
                                 sb["sT"][:, (st_base + h01 * 2) * CK:(st_base + h01 * 2 + 2) * CK],
                                 ps[gsc][:, :],
                                 sb["mask01"][:], ALU.mult))(h01, st_base),
                             reads=[gscs[h01], "mask01"],
                             writes=[f"sT.{st_base + h01 * 2}", f"sT.{st_base + h01 * 2 + 1}"],
                             tag=f"se{p_}.{h01}")

                    if prev_pair[0] is not None:
                        cxgroup(*prev_pair[0])
                    prev_pair[0] = (p_, st_base, fbs)
                cxgroup(*prev_pair[0])

                for par in range(2):
                    P.op("scalar",
                         (lambda par, cs_: lambda e: e.copy(
                             sb["Spad"][64 * par:64 * par + 64,
                                        cs_ * (H // 2) * 128:(cs_ + 1) * (H // 2) * 128]
                             .rearrange("p (a c) -> p a c", c=128)[:, :, 64 * par:64 * par + 64],
                             ps["pS"][64 * par:64 * par + 64, :]
                             .rearrange("p (a c) -> p a c", c=64)))(par, cs_),
                         reads=["pS"],
                         writes=[f"Spad.{cs_}.{2 * pp_ + par}" for pp_ in range(H // 2)],
                         tag=f"sv{par}")

                flush_D()

                dseg = []

                def make_d(lt, oslot, row0, cs_, span_lt, sp, b):
                    def emit_wo(dc, pw):
                        for pt in range(H // 2):
                            P.op("tensor",
                                 (lambda cs_, pt, lt, dc, pw: lambda e: e.matmul(
                                     ps[pw][:, :],
                                     sb["ctxT"][:, (cs_ * (H // 2) + pt) * CK + lt * 128:
                                                (cs_ * (H // 2) + pt) * CK + (lt + 1) * 128],
                                     w_slice("Wo", pt, dc * 512, (dc + 1) * 512),
                                     start=(pt == 0), stop=(pt == H // 2 - 1)))(cs_, pt, lt, dc, pw),
                                 reads=[f"ctxT.{cs_}.{pt}.0", f"ctxT.{cs_}.{pt}.1", f"Wo.{pt}"],
                                 writes=[pw], tag=f"woM{lt}{dc}")

                    def emit_y(dc, pyp):
                        P.op("tensor",
                             (lambda dc, pyp: lambda e: e.matmul(
                                 ps[pyp][:, :], sb["ones_row"][:, 0:128],
                                 sb["bias_row"][:, dc * 512:(dc + 1) * 512],
                                 start=True, stop=False))(dc, pyp),
                             reads=["ones_row", "bias_row"], writes=[pyp], tag=f"ys{lt}{dc}")
                        for di in range(NT):
                            P.op("tensor",
                                 (lambda di, dc, span_lt, pyp, sp: lambda e: e.matmul(
                                     ps[pyp][:, :],
                                     sb["xnT"][:, (sp * NT + di) * SPAN + span_lt * 128:
                                               (sp * NT + di) * SPAN + (span_lt + 1) * 128],
                                     sb["signT"][:, di * D + dc * 512: di * D + (dc + 1) * 512],
                                     start=False, stop=(di == NT - 1)))(di, dc, span_lt, pyp, sp),
                                 reads=[f"xnT.{sp}.{di}.{span_lt}"] +
                                       [f"signT.{di}.{ot}" for ot in range(NT)],
                                 writes=[pyp], tag=f"yM{lt}{dc}{di}")

                    def emit_comb(dc, pw, pyp):
                        P.op("vector",
                             (lambda oslot, dc, pyp: lambda e: e.tensor_scalar(
                                 sb["outt"][:, oslot * D + dc * 512: oslot * D + (dc + 1) * 512],
                                 ps[pyp][:, :], CLIP, -CLIP, ALU.min, ALU.max))(oslot, dc, pyp),
                             reads=[pyp], writes=[f"outt.{oslot}.{dc}"], tag=f"cl{lt}{dc}")
                        P.op("vector",
                             (lambda oslot, dc, pw: lambda e: e.tensor_tensor(
                                 sb["outt"][:, oslot * D + dc * 512: oslot * D + (dc + 1) * 512],
                                 sb["outt"][:, oslot * D + dc * 512: oslot * D + (dc + 1) * 512],
                                 ps[pw][:, :], ALU.add))(oslot, dc, pw),
                             reads=[pw, f"outt.{oslot}.{dc}"],
                             writes=[f"outt.{oslot}.{dc}"], tag=f"ad{lt}{dc}")

                    def tail():
                        P.dma("gpsimd",
                              (lambda b, row0, oslot: lambda e: e.dma_start(
                                  sb["outt"][:, oslot * D:(oslot + 1) * D],
                                  x_hbm[b, row0:row0 + 128, :], accum_op=ALU.add))(b, row0, oslot),
                              cs_acc[oslot],
                              reads=[f"outt.{oslot}.0", f"outt.{oslot}.1"],
                              writes=[f"outt.{oslot}.0", f"outt.{oslot}.1"], tag=f"acc{b}.{row0}")
                        P.dma("sync",
                              (lambda b, row0, oslot: lambda e: e.dma_start(
                                  out_hbm[b, row0:row0 + 128, :],
                                  sb["outt"][:, oslot * D:(oslot + 1) * D]))(b, row0, oslot),
                              cs_st[oslot],
                              reads=[f"outt.{oslot}.0", f"outt.{oslot}.1"], tag=f"st{b}.{row0}")
                        store_total[oslot] += 1

                    pws = {}

                    def s1():
                        pws["pw0"] = next_big()
                        emit_wo(0, pws["pw0"])

                    def s2():
                        pws["py0"] = next_big()
                        emit_y(0, pws["py0"])

                    def s3():
                        pws["pw1"] = next_big()
                        emit_wo(1, pws["pw1"])
                        emit_comb(0, pws["pw0"], pws["py0"])

                    def s4():
                        pws["py1"] = next_big()
                        emit_y(1, pws["py1"])
                        emit_comb(1, pws["pw1"], pws["py1"])
                        tail()
                    return [s1, s2, s3, s4]

                for lt in range(2):
                    oslot = o_i[0] % 4
                    o_i[0] += 1
                    dseg.extend(make_d(lt, oslot, c * CK + lt * 128, cs_, ci * 2 + lt, sp, b))

                def defer_ln(ci=ci, gspan=gspan):
                    if ci == 0 and gspan + 1 < NB * NSPAN:
                        emit_ln_transposes((gspan + 1) % 2)
                dseg.insert(0, defer_ln)
                pending_D[0] = pending_D[0] + dseg

    flush_D()

    if trunc is not None:
        P.ops = P.ops[:trunc]
        P.dma_counts = {}
        for o in P.ops:
            if o.csem is not None:
                P.dma_counts[o.csem.name] = P.dma_counts.get(o.csem.name, 0) + 1
    P.resolve()
    for i in range(4):
        n = P.dma_counts.get(cs_st[i].name, 0)
        if n:
            P.final_waits.append((cs_st[i], 16 * n))

    with nc.allow_low_precision(reason="bf16 matmul operand rounding"):
        with nc.Block() as block:
            P.emit(block)
    return nc


def kernel(L0_attn_x, L0_attn_Wq, L0_attn_Wk, L0_attn_Wv, L0_attn_Wo,
           L0_attn_Wb, L0_attn_bb, L0_ffn_weights, L0_ffn_scale, L0_ffn_bias):
    B, L, D_ = L0_attn_x.shape
    n_cores = 8
    nb = B // n_cores
    nc = build_program(NB=nb, L=L)

    x = np.ascontiguousarray(np.asarray(L0_attn_x, dtype=np.float32))
    common = dict(
        wq=np.ascontiguousarray(L0_attn_Wq, dtype=np.float32),
        wk=np.ascontiguousarray(L0_attn_Wk, dtype=np.float32),
        wv=np.ascontiguousarray(L0_attn_Wv, dtype=np.float32),
        wo=np.ascontiguousarray(L0_attn_Wo, dtype=np.float32),
        wb=np.ascontiguousarray(L0_attn_Wb, dtype=np.float32),
        bb=np.ascontiguousarray(np.asarray(L0_attn_bb, dtype=np.float32).reshape(1, H)),
        wf=np.ascontiguousarray(L0_ffn_weights, dtype=np.float32),
        fsc=np.ascontiguousarray(np.asarray(L0_ffn_scale, dtype=np.float32).reshape(1, D)),
        fbi=np.ascontiguousarray(np.asarray(L0_ffn_bias, dtype=np.float32).reshape(1, D)),
    )
    in_maps = []
    for i in range(n_cores):
        m = dict(common)
        m["x"] = np.ascontiguousarray(x[i * nb:(i + 1) * nb])
        in_maps.append(m)

    res = run_bass_kernel_spmd(nc, in_maps, core_ids=list(range(n_cores)))
    return np.concatenate([r["out"] for r in res.results], axis=0)
